# version 06
# baseline (speedup 1.0000x reference)
import contextlib
import numpy as np

import concourse.bass as bass
import concourse.mybir as mybir
from concourse.bass_utils import run_bass_kernel_spmd
from concourse.aot_env import allow_ici_all_to_all

F32 = mybir.dt.float32
BF16 = mybir.dt.bfloat16
I32 = mybir.dt.int32
F8 = mybir.dt.float8e4
ALU = mybir.AluOpType
AF = mybir.ActivationFunctionType
AX = mybir.AxisListType.X

NCORES = 8
B = 16
TL = 512
D = 1024
NI = 4096
NT = B * TL // 128
GAMMA = 0.96875
EPS = 1e-5

CH = {"t": "ct", "v": "cv", "s": "cs", "g": "cg"}
SERIAL = ("v", "s", "g")


class Plan:

    def __init__(self):
        self.q = {e: [] for e in "tvsgy"}
        self.cnt = {}
        self.pending = {e: [] for e in "tvsgy"}

    def tick(self, sem, amt):
        self.cnt[sem] = self.cnt.get(sem, 0) + amt
        return (sem, self.cnt[sem])

    def add(self, eng, fn, waits=(), sem=None, amt=1):
        w = list(self.pending[eng]) + list(waits)
        self.pending[eng] = []
        self.q[eng].append((tuple(w), fn, sem, amt))
        return self.tick(sem, amt) if sem else None

    def c(self, eng, fn, waits=()):
        w = list(waits)
        if eng in SERIAL and self.cnt.get(CH[eng], 0) > 0:
            w.append((CH[eng], self.cnt[CH[eng]]))
        return self.add(eng, fn, w, CH[eng], 1)

    def d(self, eng, fn, stream, waits=()):
        return self.add(eng, fn, waits, "D" + stream, 16)

    def total(self, stream):
        return ("D" + stream, self.cnt.get("D" + stream, 0))

    def barrier(self, tickets):
        for e in self.q:
            self.pending[e].extend(tickets)


def build_nc():
    nc = bass.Bass("TRN2", target_bir_lowering=False)
    d_x = nc.dram_tensor("x", [B, TL, D], F32, kind="ExternalInput").ap()
    d_wq = nc.dram_tensor("wq", [D, D], F32, kind="ExternalInput").ap()
    d_wk = nc.dram_tensor("wk", [D, D], F32, kind="ExternalInput").ap()
    d_wv = nc.dram_tensor("wv", [D, D], F32, kind="ExternalInput").ap()
    d_wgT = nc.dram_tensor("wgT", [D, NI], F32, kind="ExternalInput").ap()
    d_wuT = nc.dram_tensor("wuT", [D, NI], F32, kind="ExternalInput").ap()
    d_wdT = nc.dram_tensor("wdT", [NI, D], F32, kind="ExternalInput").ap()
    d_oh = nc.dram_tensor("oh", [1, 4], F32, kind="ExternalInput").ap()
    d_out = nc.dram_tensor("out", [B, TL, D], F32, kind="ExternalOutput").ap()
    h_ccin = nc.dram_tensor("cc_in", [4096, TL], BF16)
    h_ccout = nc.dram_tensor("cc_out", [4096, TL], BF16)
    h_scr8 = nc.dram_tensor("scr8", [8, 128], F32)
    d_ccin = h_ccin.ap()
    d_ccout = h_ccout.ap()

    P = Plan()
    es = contextlib.ExitStack()
    with es:
        sb = lambda n, s, d, **k: es.enter_context(nc.sbuf_tensor(n, s, d, **k))
        tg = sb("tg", [128, 8 * NI], F8)
        tu = sb("tu", [128, 8 * NI], F8)
        td = sb("td", [128, 32 * D], F8)
        wqb = sb("wqb", [128, 8 * D], BF16)
        st_used = sb("st_used", [128, 4 * D], BF16)
        mat_cols = sb("mat_cols", [128, 8], F32)
        epscol = sb("epscol", [128, 1], F32)

        pb = [es.enter_context(nc.psum_tensor("pb%d" % i, [128, 512], F32))
              for i in range(8)]
        sems = {}

        es1 = contextlib.ExitStack()
        with es1:
            sb1 = lambda n, s, d, **k: es1.enter_context(nc.sbuf_tensor(n, s, d, **k))
            wvs_bc = sb1("wvs_bc", [128, D], F32)
            ident = sb1("ident", [128, 128], F32)
            oh_col = sb1("oh_col", [128, 4], F32)
            ones1 = sb1("ones1", [1, 128], F32)
            onescol = sb1("onescol", [128, 1], F32)
            ii32 = sb1("ii32", [128, 128], I32)
            rwsw_row = sb1("rwsw_row", [1, 8], F32)
            ohrow = sb1("ohrow", [1, 4], F32)
            row_sb = sb1("row_sb", [1, D], F32)
            gam512 = sb1("gam512", [128, 512], BF16)
            sum11 = sb1("sum11", [1, 4], F32)
            xz = [sb1("xz%d" % i, [128, D], F32) for i in range(2)]
            xnz = sb1("xnz", [128, D], F32)
            vtmp = sb1("vtmp", [128, D], F32)
            zacc = sb1("zacc", [128, 4 * D], F32)
            wld = [sb1("wld%d" % i, [128, D], F32) for i in range(2)]
            wqp = sb1("wqp", [128, D], F32)
            gqy = sb1("gqy", [128, D], F32)
            gqp = sb1("gqp", [128, D], F32)
            wsumV = sb1("wsumV", [128, 1], F32)
            zT = sb1("zT", [128, 8 * 512], BF16)
            wkb = [sb1("wkb%d" % i, [128, D], BF16) for i in range(2)]
            mTp = [sb1("mTp%d" % i, [128, 512], BF16) for i in range(2)]
            mTw = [sb1("mTw%d" % i, [128, 512], BF16) for i in range(2)]
            STp = [sb1("STp%d" % i, [128, 512], F32) for i in range(2)]
            carry = sb1("carry", [128, 1], F32)
            zcols = sb1("zcols", [128, 24], F32)
            wvs_st = sb1("wvs_st", [128, 8], F32)
            wqy = vtmp
            sqscr = gqp

            t_eps = P.c("v", lambda e: e.memset(epscol[:, :], EPS))
            P.c("v", lambda e: e.memset(rwsw_row[:, :], 0.0))
            t_ones1 = P.c("v", lambda e: e.memset(ones1[:, :], 1.0))
            t_onesc = P.c("v", lambda e: e.memset(onescol[:, :], 1.0))
            P.c("v", lambda e: e.memset(gam512[:, :], GAMMA))
            t_z0 = P.c("v", lambda e: e.memset(zacc[:, :], 0.0))
            P.c("v", lambda e: e.memset(wsumV[:, :], 0.0))
            t_ii = P.c("g", lambda e: e.iota(ii32[:, :], [[1, 128]], base=0,
                                             channel_multiplier=-1))
            t_id = P.c("g", lambda e: e.tensor_scalar(
                ident[:, :], ii32[:, :], 0.0, None, ALU.is_equal), waits=[t_ii])
            t_ohl = P.d("y", lambda e: e.dma_start(out=ohrow[:, :], in_=d_oh),
                        "oh")

            lastc = {0: None, 1: None}
            for j in range(8):
                p = j % 2
                w = [lastc[p]] if lastc[p] else []
                t_l = P.d("y", (lambda e, j=j, p=p: e.dma_start(
                    out=wld[p][:, :], in_=d_wq[j * 128:(j + 1) * 128, :])),
                    "w%d" % p, waits=w)
                lastc[p] = P.c("v", (lambda e, j=j, p=p: e.tensor_copy(
                    wqb[:, j * D:(j + 1) * D], wld[p][:, :])), waits=[t_l])
            for j in range(8):
                p = j % 2
                t_l = P.d("y", (lambda e, j=j, p=p: e.dma_start(
                    out=wld[p][:, :], in_=d_wv[j * 128:(j + 1) * 128, :])),
                    "w%d" % p, waits=[lastc[p]])
                lastc[p] = P.c("v", (lambda e, j=j, p=p: e.reduce_sum(
                    wvs_st[:, j:j + 1], wld[p][:, :], AX)), waits=[t_l])
            t_s8 = P.d("y", lambda e: e.dma_start(
                out=h_scr8.ap().rearrange("a b -> b a"), in_=wvs_st[:, :],
                allow_slow_non_contiguous=True),
                "s8", waits=[lastc[0], lastc[1]])
            t_row = P.d("y", lambda e: e.dma_start(
                out=row_sb[:, :], in_=h_scr8.ap().rearrange("a b -> (a b)")[None, :]),
                "row", waits=[t_s8])
            t_b1 = P.c("t", lambda e: e.matmul(pb[0][:, :], ones1[0:1, :],
                                               row_sb[0:1, 0:512], start=True,
                                               stop=True), waits=[t_row, t_ones1])
            t_b2 = P.c("t", lambda e: e.matmul(pb[1][:, :], ones1[0:1, :],
                                               row_sb[0:1, 512:1024], start=True,
                                               stop=True), waits=[t_row, t_ones1])
            t_wbc1 = P.c("s", lambda e: e.copy(wvs_bc[:, 0:512], pb[0][:, :]),
                         waits=[t_b1])
            t_wbc2 = P.c("s", lambda e: e.copy(wvs_bc[:, 512:1024], pb[1][:, :]),
                         waits=[t_b2])
            t_b3 = P.c("t", lambda e: e.matmul(pb[2][:, 0:4], ones1[0:1, :],
                                               ohrow[0:1, :], start=True, stop=True, skip_group_check=True),
                       waits=[t_ohl, t_ones1])
            t_ohc = P.c("s", lambda e: e.copy(oh_col[:, :], pb[2][:, 0:4]),
                        waits=[t_b3])

            mats = [(d_wgT, 8, 4), (d_wuT, 8, 4), (d_wdT, 32, 1)]
            accV, accG = t_z0, None
            t_s11_prev = None
            t_sw = None
            for mi, (dm, nj, nq) in enumerate(mats):
                for ci in range(nj * nq):
                    j, qq = divmod(ci, nq)
                    p = ci % 2
                    t_l = P.d("y", (lambda e, dm=dm, j=j, qq=qq, p=p: e.dma_start(
                        out=wld[p][:, :],
                        in_=dm[j * 128:(j + 1) * 128, qq * D:(qq + 1) * D])),
                        "w%d" % p, waits=[lastc[p]] if lastc[p] else [])
                    scr = vtmp if p == 0 else gqy
                    t_r = P.c("v", (lambda e, p=p, scr=scr: e.reduce_sum(
                        scr[:, 0:1], wld[p][:, :], AX, apply_absolute_value=True)),
                        waits=[t_l])
                    t_a = P.c("v", (lambda e, scr=scr: e.tensor_add(
                        wsumV[:, :], wsumV[:, :], scr[:, 0:1])), waits=[t_r])
                    lastc[p] = t_r
                    accV = t_a
                t_cmb = accV
                wmm = [t_cmb, t_onesc]
                if t_s11_prev:
                    wmm.append(t_s11_prev)
                t_mm = P.c("t", (lambda e, mi=mi: e.matmul(
                    pb[6][0:1, mi:mi + 1], wsumV[:, 0:1], onescol[:, :],
                    start=True, stop=True, skip_group_check=True)), waits=wmm)
                t_s11 = P.c("v", (lambda e, mi=mi: e.tensor_copy(
                    sum11[0:1, mi:mi + 1], pb[6][0:1, mi:mi + 1])), waits=[t_mm])
                t_s11_prev = t_s11
                t_rw = P.c("v", (lambda e, mi=mi: e.tensor_scalar(
                    rwsw_row[0:1, 2 * mi:2 * mi + 1], sum11[0:1, mi:mi + 1],
                    float(2.0 ** -22), 1e-5, ALU.mult, ALU.max)), waits=[t_s11])
                t_sw = P.c("v", (lambda e, mi=mi: e.reciprocal(
                    rwsw_row[0:1, 2 * mi + 1:2 * mi + 2],
                    rwsw_row[0:1, 2 * mi:2 * mi + 1])), waits=[t_rw])
                accV = P.c("v", lambda e: e.memset(wsumV[:, :], 0.0), waits=[t_mm])
            t_bmm = P.c("t", lambda e: e.matmul(
                pb[2][:, 0:8], ones1[0:1, :], rwsw_row[0:1, :],
                start=True, stop=True, skip_group_check=True), waits=[t_sw, t_ohc])
            t_mcols = P.c("s", lambda e: e.copy(mat_cols[:, :], pb[2][:, 0:8]),
                          waits=[t_bmm])

            for mi, (dm, nj, nq) in enumerate(mats):
                dst = (tg, tu, td)[mi]
                for ci in range(nj * nq):
                    j, qq = divmod(ci, nq)
                    p = ci % 2
                    eng = "v" if p == 0 else "g"
                    y = wqy if eng == "v" else gqy
                    pp = wqp if eng == "v" else gqp
                    t_l = P.d("y", (lambda e, dm=dm, j=j, qq=qq, p=p: e.dma_start(
                        out=wld[p][:, :],
                        in_=dm[j * 128:(j + 1) * 128, qq * D:(qq + 1) * D])),
                        "w%d" % p, waits=[lastc[p]] if lastc[p] else [])
                    t1 = P.c(eng, (lambda e, p=p, y=y, mi=mi: e.tensor_scalar(
                        y[:, :], wld[p][:, :], mat_cols[:, 2 * mi + 1:2 * mi + 2],
                        None, ALU.mult)), waits=[t_l, t_mcols])
                    P.c(eng, (lambda e, y=y, pp=pp: e.tensor_scalar(
                        pp[:, :], y[:, :], 0.5, None, ALU.is_gt)), waits=[t1])
                    P.c(eng, (lambda e, y=y: e.tensor_scalar(
                        y[:, :], y[:, :], -1.0, 0.5, ALU.mult, ALU.is_gt)))
                    if nq == 4:
                        sl = (lambda dst=dst, j=j, qq=qq:
                              dst[:, j * NI + qq * D: j * NI + (qq + 1) * D])
                    else:
                        sl = (lambda dst=dst, j=j: dst[:, j * D:(j + 1) * D])
                    t4 = P.c(eng, (lambda e, sl=sl, pp=pp, y=y: e.tensor_sub(
                        sl(), pp[:, :], y[:, :])))
                    lastc[p] = t4

            t_sq = None
            t_zstt = t_z0
            xread = {0: None, 1: None}
            for n in range(NT):
                b, tc = divmod(n, 4)
                p = n % 2
                w = [xread[p]] if xread[p] else []
                t_xl = P.d("y", (lambda e, b=b, tc=tc, p=p: e.dma_start(
                    out=xz[p][:, :], in_=d_x[b, tc * 128:(tc + 1) * 128, :])),
                    "xz%d" % p, waits=w)
                t_s1 = P.c("v", (lambda e, p=p: e.reduce_sum(
                    zcols[:, 0:1], xz[p][:, :], AX)), waits=[t_xl])
                t_s2 = P.c("s", (lambda e, p=p: e.activation(
                    sqscr[:, :], xz[p][:, :], AF.Square,
                    accum_out=zcols[:, 7:8])),
                    waits=[t_xl, ("cg", P.cnt["cg"])])
                P.c("v", lambda e: e.tensor_scalar(
                    zcols[:, 1:2], zcols[:, 0:1], float(1.0 / D), None, ALU.mult),
                    waits=[t_s1])
                P.c("v", lambda e: e.tensor_mul(
                    zcols[:, 2:3], zcols[:, 1:2], zcols[:, 1:2]))
                t_vr = P.c("v", lambda e: e.scalar_tensor_tensor(
                    zcols[:, 3:4], zcols[:, 7:8], float(1.0 / D), zcols[:, 2:3],
                    ALU.mult, ALU.subtract), waits=[t_s2])
                t_sd = P.c("s", lambda e: e.activation(
                    zcols[:, 4:5], zcols[:, 3:4], AF.Sqrt,
                    bias=epscol[:, 0:1]), waits=[t_vr, t_eps])
                t_rs = P.c("v", lambda e: e.reciprocal(
                    zcols[:, 5:6], zcols[:, 4:5]), waits=[t_sd])
                t_nm = P.c("v", lambda e: e.scalar_tensor_tensor(
                    zcols[:, 6:7], zcols[:, 1:2], -1.0, zcols[:, 5:6],
                    ALU.mult, ALU.mult))
                t_xn = P.c("s", (lambda e, p=p: e.activation(
                    xnz[:, :], xz[p][:, :], AF.Identity,
                    bias=zcols[:, 6:7], scale=zcols[:, 5:6])),
                    waits=[t_nm, t_zstt])
                xread[p] = t_xn
                P.c("v", lambda e: e.tensor_mul(
                    vtmp[:, :], xnz[:, :], wvs_bc[:, :]),
                    waits=[t_xn, t_wbc1, t_wbc2])
                t_vs = P.c("v", lambda e: e.reduce_sum(
                    zcols[:, 8:9], vtmp[:, :], AX))
                t_zstt = P.c("v", (lambda e, tc=tc: e.scalar_tensor_tensor(
                    zacc[:, tc * D:(tc + 1) * D], xnz[:, :], zcols[:, 8:9],
                    zacc[:, tc * D:(tc + 1) * D], ALU.mult, ALU.add)), waits=[t_vs])

            t_ev = {0: None, 1: None}
            for j in range(8):
                for tc in range(4):
                    bk = (j * 4 + tc) % 2
                    w = [t_zstt, t_id, t_wbc1, t_wbc2]
                    if t_ev[bk]:
                        w.append(t_ev[bk])
                    t_tr = P.c("t", (lambda e, j=j, tc=tc, bk=bk: e.transpose(
                        pb[bk][:, 0:128],
                        zacc[:, tc * D + j * 128: tc * D + (j + 1) * 128],
                        ident[:, :])), waits=w)
                    t_ev[bk] = P.c("v", (lambda e, j=j, tc=tc, bk=bk: e.tensor_copy(
                        zT[:, j * 512 + tc * 128: j * 512 + (tc + 1) * 128],
                        pb[bk][:, 0:128])), waits=[t_tr])
            zt_all = [t_ev[0], t_ev[1]]
            mm_last = []
            wkc = {0: None, 1: None}
            t_m = None
            for j in range(8):
                p = j % 2
                t_l = P.d("y", (lambda e, j=j, p=p: e.dma_start(
                    out=wld[p][:, :], in_=d_wk[j * 128:(j + 1) * 128, :])),
                    "w%d" % p, waits=[lastc[p]] if lastc[p] else [])
                wc = [t_l]
                if wkc[p]:
                    wc.append(wkc[p])
                t_c = P.c("v", (lambda e, p=p: e.tensor_copy(
                    wkb[p][:, :], wld[p][:, :])), waits=wc)
                lastc[p] = t_c
                for k in range(8):
                    wts = [t_c]
                    if j == 0:
                        wts += zt_all + [t_mcols, t_s11_prev]
                    t_m = P.c("t", (lambda e, j=j, k=k, p=p: e.matmul(
                        pb[k][:, :], wkb[p][:, k * 128:(k + 1) * 128],
                        zT[:, j * 512:(j + 1) * 512],
                        start=(j == 0), stop=(j == 7), skip_group_check=True)), waits=wts)
                    if j == 7:
                        mm_last.append(t_m)
                wkc[p] = t_m
            ccin_dmas = []
            mtp_free = {0: None, 1: None}
            for k in range(8):
                p = k % 2
                w = [mm_last[k]]
                if mtp_free[p]:
                    w.append(mtp_free[p])
                t_e = P.c("v", (lambda e, k=k, p=p: e.tensor_copy(
                    mTp[p][:, :], pb[k][:, :])), waits=w)
                last_d = None
                for ji in range(4):
                    jdst = (k // 4) + 2 * ji
                    rr = 512 * jdst + (128 * k) % 512
                    last_d = P.d("y", (lambda e, rr=rr, p=p: e.dma_start(
                        out=d_ccin[rr:rr + 128, :], in_=mTp[p][:, :])),
                        "ci%d" % p, waits=[t_e])
                mtp_free[p] = last_d
                ccin_dmas.append(last_d)
            t_cc = P.add("g", (lambda e: e.collective_compute(
                "AllToAll", ALU.bypass,
                replica_groups=[list(range(NCORES))],
                ins=[d_ccin.opt()], outs=[d_ccout.opt()])),
                waits=ccin_dmas + [("cg", P.cnt["cg"])], sem="cc", amt=1)

            sel_last = None
            mtw_free = {0: None, 1: None}
            stp_free = {0: None, 1: None}
            for k in range(4):
                t_cy = P.c("v", lambda e: e.memset(carry[:, :], 0.0))
                for cp in range(8):
                    p = cp % 2
                    w = [t_cc]
                    if mtw_free[p]:
                        w.append(mtw_free[p])
                    t_l = P.d("y", (lambda e, k=k, cp=cp, p=p: e.dma_start(
                        out=mTw[p][:, :],
                        in_=d_ccout[512 * cp + 128 * k: 512 * cp + 128 * k + 128, :])),
                        "mw%d" % p, waits=w)
                    wscan = [t_l, t_cy]
                    if cp == 0:
                        wscan.append(P.c("v", (lambda e, p=p: e.memset(
                            mTw[p][:, 0:1], 0.0)), waits=[t_l]))
                    if stp_free[p]:
                        wscan.append(stp_free[p])
                    t_sc = P.c("v", (lambda e, p=p: e.tensor_tensor_scan(
                        STp[p][:, :], gam512[:, :], mTw[p][:, :], carry[:, 0:1],
                        ALU.mult, ALU.add)), waits=wscan)
                    mtw_free[p] = t_sc
                    t_cy = P.c("v", (lambda e, p=p: e.tensor_copy(
                        carry[:, :], STp[p][:, 511:512])), waits=[t_sc])
                    if cp == 0:
                        P.c("v", (lambda e, p=p: e.tensor_copy(
                            STp[p][:, 0:1], STp[p][:, 1:2])), waits=[t_sc])
                    dst = (lambda k=k, cp=cp:
                           st_used[:, k * D + 128 * cp: k * D + 128 * cp + 128])
                    t_sel = P.c("v", (lambda e, p=p, dst=dst: e.tensor_scalar(
                        dst(), STp[p][:, 0::4], oh_col[:, 0:1], None, ALU.mult)),
                        waits=[t_ohc])
                    for qq in range(1, 4):
                        t_sel = P.c("v", (lambda e, p=p, qq=qq, dst=dst:
                                          e.scalar_tensor_tensor(
                            dst(), STp[p][:, qq::4], oh_col[:, qq:qq + 1], dst(),
                            ALU.mult, ALU.add)))
                    stp_free[p] = t_sel
                    sel_last = t_sel

            barrier = [("cv", P.cnt["cv"]), ("cs", P.cnt["cs"]),
                       ("ct", P.cnt["ct"]), ("cg", P.cnt["cg"]), ("cc", 1)]
            barrier += [P.total(s) for s in
                        ["oh", "s8", "row", "w0", "w1", "xz0", "xz1",
                         "ci0", "ci1", "mw0", "mw1"]]
            P.barrier(barrier)

        es2 = contextlib.ExitStack()
        with es2:
            sb2 = lambda n, s, d, **k: es2.enter_context(nc.sbuf_tensor(n, s, d, **k))
            xb = [sb2("xb%d" % i, [128, D], F32) for i in range(2)]
            xn = [sb2("xn%d" % i, [128, D], F32) for i in range(2)]
            xnb = [sb2("xnb%d" % i, [128, D], BF16) for i in range(2)]
            xnT = [sb2("xnT%d" % i, [128, D], BF16, is_transpose_dest=True)
                   for i in range(2)]
            kq = [sb2("kq%d" % i, [128, D], BF16) for i in range(2)]
            kqT = [sb2("kqT%d" % i, [128, D], BF16, is_transpose_dest=True)
                   for i in range(2)]
            qt = [sb2("qt%d" % i, [128, D], BF16) for i in range(2)]
            gt = [sb2("gt%d" % i, [128, D], BF16) for i in range(2)]
            tmp = [sb2("tmp%d" % i, [128, D], BF16) for i in range(2)]
            hh = sb2("hh", [128, NI], BF16)
            kh = sb2("kh", [128, NI], BF16)
            khT = sb2("khT", [128, NI], BF16, is_transpose_dest=True)
            ysc = sb2("ysc", [128, D], F32)
            isc = sb2("isc", [128, D], I32)
            sq2 = ysc
            gsil = [sb2("gsil%d" % i, [128, 512], F32) for i in range(2)]
            cols = [sb2("cols%d" % i, [128, 24], F32) for i in range(2)]

            t_outd = {0: None, 1: None}
            t_sq2 = None
            t_comb = {0: None, 1: None}
            t_qevac = {0: None, 1: None}
            t_gevac = {0: None, 1: None}
            t_gsil = {}
            t_hstt = {}
            t_quantG = None
            t_quantV = None
            t_khT = None
            t_xnT = {0: None, 1: None}
            t_kqT = {0: None, 1: None}
            t_down = None
            t_tmp2 = {0: None, 1: None}
            t_xnact = {0: None, 1: None}
            qcons = {0: None, 1: None}
            gucons = {0: None, 1: None}

            for n in range(NT):
                b, tc = divmod(n, 4)
                p = n % 2
                w = [t_outd[p]] if t_outd[p] else []
                t_xl = P.d("y", (lambda e, b=b, tc=tc, p=p: e.dma_start(
                    out=xb[p][:, :], in_=d_x[b, tc * 128:(tc + 1) * 128, :])),
                    "xb%d" % p, waits=w)
                t_s1 = P.c("v", (lambda e, p=p: e.reduce_sum(
                    cols[p][:, 0:1], xb[p][:, :], AX)), waits=[t_xl])
                wsq = [t_xl]
                if t_quantV:
                    wsq.append(t_quantV)
                t_s2 = P.c("s", (lambda e, p=p: e.activation(
                    sq2[:, :], xb[p][:, :], AF.Square,
                    accum_out=cols[p][:, 7:8])), waits=wsq)
                t_sq2 = t_s2
                P.c("v", (lambda e, p=p: e.tensor_scalar(
                    cols[p][:, 1:2], cols[p][:, 0:1], float(1.0 / D), None,
                    ALU.mult)), waits=[t_s1])
                P.c("v", (lambda e, p=p: e.tensor_mul(
                    cols[p][:, 2:3], cols[p][:, 1:2], cols[p][:, 1:2])))
                t_vr = P.c("v", (lambda e, p=p: e.scalar_tensor_tensor(
                    cols[p][:, 3:4], cols[p][:, 7:8], float(1.0 / D),
                    cols[p][:, 2:3], ALU.mult, ALU.subtract)), waits=[t_s2])
                t_sd = P.c("s", (lambda e, p=p: e.activation(
                    cols[p][:, 4:5], cols[p][:, 3:4], AF.Sqrt,
                    bias=epscol[:, 0:1])), waits=[t_vr, t_eps])
                t_rs = P.c("v", (lambda e, p=p: e.reciprocal(
                    cols[p][:, 5:6], cols[p][:, 4:5])), waits=[t_sd])
                t_nm = P.c("v", (lambda e, p=p: e.scalar_tensor_tensor(
                    cols[p][:, 6:7], cols[p][:, 1:2], -1.0, cols[p][:, 5:6],
                    ALU.mult, ALU.mult)))
                wxn = [t_nm]
                if t_quantV:
                    wxn.append(t_quantV)
                t_xna = P.c("s", (lambda e, p=p: e.activation(
                    xn[p][:, :], xb[p][:, :], AF.Identity,
                    bias=cols[p][:, 6:7], scale=cols[p][:, 5:6])), waits=wxn)
                t_xnact[p] = t_xna
                wb = [t_xna]
                if t_xnT[p]:
                    wb.append(t_xnT[p])
                t_xnb = P.c("v", (lambda e, p=p: e.tensor_copy(
                    xnb[p][:, :], xn[p][:, :])), waits=wb)
                t_mx = P.c("v", (lambda e, p=p: e.reduce_max(
                    cols[p][:, 8:9], xn[p][:, :], AX, apply_absolute_value=True)),
                    waits=[t_xna])
                P.c("v", (lambda e, p=p: e.tensor_scalar(
                    cols[p][:, 9:10], cols[p][:, 8:9], 1e-5, None, ALU.max)),
                    waits=[t_mx])
                P.c("v", (lambda e, p=p: e.reciprocal(
                    cols[p][:, 10:11], cols[p][:, 9:10])))
                t_sx = P.c("v", (lambda e, p=p: e.tensor_scalar(
                    cols[p][:, 11:12], cols[p][:, 10:11], 127.0, None, ALU.mult)))
                P.c("v", (lambda e, p=p: e.tensor_scalar(
                    cols[p][:, 12:13], cols[p][:, 9:10], float(1.0 / 127.0), None,
                    ALU.mult)))
                t_rxg = P.c("v", (lambda e, p=p: e.tensor_mul(
                    cols[p][:, 13:14], cols[p][:, 12:13], mat_cols[:, 0:1])))
                t_rxu = P.c("v", (lambda e, p=p: e.tensor_mul(
                    cols[p][:, 14:15], cols[p][:, 12:13], mat_cols[:, 2:3])))
                wq1 = [t_sx]
                if t_quantV:
                    wq1.append(t_quantV)
                t_q1 = P.c("v", (lambda e, p=p: e.tensor_scalar(
                    ysc[:, :], xn[p][:, :], cols[p][:, 11:12], 0.5, ALU.mult,
                    ALU.add)), waits=wq1)
                t_q2 = P.c("v", lambda e: e.tensor_copy(isc[:, :], ysc[:, :]),
                           waits=[t_q1])
                t_q3 = P.c("v", lambda e: e.tensor_tensor(
                    ysc[:, :], isc[:, :], ysc[:, :], ALU.is_gt), waits=[t_q2])
                wq4 = [t_q3]
                if t_kqT[p]:
                    wq4.append(t_kqT[p])
                t_q4 = P.c("v", (lambda e, p=p: e.tensor_sub(
                    kq[p][:, :], isc[:, :], ysc[:, :])), waits=wq4)
                t_quantV = t_q4
                last_t = None
                for j in range(8):
                    wdt = [t_xnb]
                    if j == 0 and qcons[p]:
                        wdt.append(qcons[p])
                    last_t = P.d("s", (lambda e, p=p, j=j: e.dma_start(
                        out=xnT[p][:, j * 128:(j + 1) * 128],
                        in_=xnb[p][:, j * 128:(j + 1) * 128], transpose=True)),
                        "xt%d" % p, waits=wdt)
                t_xnT[p] = last_t
                for j in range(8):
                    wdt = [t_q4]
                    if j == 0 and gucons[p]:
                        wdt.append(gucons[p])
                    last_t = P.d("s", (lambda e, p=p, j=j: e.dma_start(
                        out=kqT[p][:, j * 128:(j + 1) * 128],
                        in_=kq[p][:, j * 128:(j + 1) * 128], transpose=True)),
                        "kq%d" % p, waits=wdt)
                t_kqT[p] = last_t
                t_qmm = []
                for j in range(8):
                    wts = []
                    if j == 0:
                        wts = [t_xnT[p]]
                        for bi in (0, 1):
                            if t_qevac[bi]:
                                wts.append(t_qevac[bi])
                    for bi in (0, 1):
                        t_q = P.c("t", (lambda e, p=p, j=j, bi=bi: e.matmul(
                            pb[bi][:, :], xnT[p][:, j * 128:(j + 1) * 128],
                            wqb[:, j * D + bi * 512: j * D + (bi + 1) * 512],
                            start=(j == 0), stop=(j == 7), skip_group_check=True)), waits=wts)
                        wts = []
                        if j == 7:
                            t_qmm.append(t_q)
                qcons[p] = t_qmm[1]
                for bi in (0, 1):
                    wts = [t_qmm[bi]]
                    if t_comb[p]:
                        wts.append(t_comb[p])
                    t_qevac[bi] = P.c("s", (lambda e, p=p, bi=bi: e.copy(
                        qt[p][:, bi * 512:(bi + 1) * 512], pb[bi][:, :])),
                        waits=wts)
                for oc in range(8):
                    bg = 2 + (oc % 2)
                    bu = 4 + (oc % 2)
                    wts = [t_kqT[p]]
                    if t_gsil.get(oc % 2):
                        wts.append(t_gsil[oc % 2])
                    if t_hstt.get(oc % 2):
                        wts.append(t_hstt[oc % 2])
                    if oc < 2:
                        if t_quantG:
                            wts.append(t_quantG)
                        if t_quantV:
                            wts.append(t_quantV)
                    t_gu = None
                    for j in range(8):
                        P.c("t", (lambda e, p=p, j=j, oc=oc, bg=bg: e.matmul(
                            pb[bg][:, :], kqT[p][:, j * 128:(j + 1) * 128],
                            tg[:, j * NI + oc * 512: j * NI + (oc + 1) * 512],
                            start=(j == 0), stop=(j == 7), skip_group_check=True)),
                            waits=wts if j == 0 else [])
                        t_gu = P.c("t", (lambda e, p=p, j=j, oc=oc, bu=bu: e.matmul(
                            pb[bu][:, :], kqT[p][:, j * 128:(j + 1) * 128],
                            tu[:, j * NI + oc * 512: j * NI + (oc + 1) * 512],
                            start=(j == 0), stop=(j == 7), skip_group_check=True)))
                    t_gs = P.c("s", (lambda e, p=p, oc=oc, bg=bg: e.activation(
                        gsil[oc % 2][:, :], pb[bg][:, :], AF.Silu,
                        scale=cols[p][:, 13:14])), waits=[t_gu, t_rxg])
                    t_gsil[oc % 2] = t_gs
                    t_hs = P.c("v", (lambda e, p=p, oc=oc, bu=bu:
                                     e.scalar_tensor_tensor(
                        hh[:, oc * 512:(oc + 1) * 512], pb[bu][:, :],
                        cols[p][:, 14:15], gsil[oc % 2][:, :], ALU.mult,
                        ALU.mult)), waits=[t_gs, t_rxu])
                    t_hstt[oc % 2] = t_hs
                gucons[p] = t_gu
                t_mh = P.c("v", (lambda e, p=p: e.reduce_max(
                    cols[p][:, 15:16], hh[:, :], AX, apply_absolute_value=True)),
                    waits=[t_hstt[0], t_hstt[1]])
                P.c("v", (lambda e, p=p: e.tensor_scalar(
                    cols[p][:, 16:17], cols[p][:, 15:16], 1e-5, None, ALU.max)),
                    waits=[t_mh])
                P.c("v", (lambda e, p=p: e.reciprocal(
                    cols[p][:, 17:18], cols[p][:, 16:17])))
                t_sh = P.c("v", (lambda e, p=p: e.tensor_scalar(
                    cols[p][:, 18:19], cols[p][:, 17:18], 127.0, None, ALU.mult)))
                P.c("v", (lambda e, p=p: e.tensor_scalar(
                    cols[p][:, 19:20], cols[p][:, 16:17], float(1.0 / 127.0),
                    None, ALU.mult)))
                t_rhd = P.c("v", (lambda e, p=p: e.tensor_mul(
                    cols[p][:, 20:21], cols[p][:, 19:20], mat_cols[:, 4:5])))
                qticks = []
                for q in range(4):
                    wts = [t_sh]
                    if t_khT:
                        wts.append(t_khT)
                    a1 = P.c("v", (lambda e, p=p, q=q: e.tensor_scalar(
                        ysc[:, :], hh[:, q * D:(q + 1) * D], cols[p][:, 18:19],
                        0.5, ALU.mult, ALU.add)), waits=wts)
                    a2 = P.c("v", lambda e: e.tensor_copy(
                        isc[:, :], ysc[:, :]), waits=[a1])
                    a3 = P.c("v", lambda e: e.tensor_tensor(
                        ysc[:, :], isc[:, :], ysc[:, :], ALU.is_gt), waits=[a2])
                    a4 = P.c("v", (lambda e, q=q: e.tensor_sub(
                        kh[:, q * D:(q + 1) * D], isc[:, :], ysc[:, :])),
                        waits=[a3])
                    qticks.append(a4)
                t_quantG = qticks[1]
                t_quantV = qticks[3]
                last_t = None
                for j in range(32):
                    wts = [qticks[j // 8]]
                    if j == 0 and t_down:
                        wts.append(t_down)
                    last_t = P.d("s", (lambda e, j=j: e.dma_start(
                        out=khT[:, j * 128:(j + 1) * 128],
                        in_=kh[:, j * 128:(j + 1) * 128], transpose=True)),
                        "kh", waits=wts)
                t_khT = last_t
                t_dmm = []
                for oc in range(32):
                    wts = []
                    if oc == 0:
                        wts = [t_khT]
                        for bi in (0, 1):
                            if t_gevac[bi]:
                                wts.append(t_gevac[bi])
                    for bi in (0, 1):
                        t_d = P.c("t", (lambda e, oc=oc, bi=bi: e.matmul(
                            pb[6 + bi][:, :], khT[:, oc * 128:(oc + 1) * 128],
                            td[:, oc * D + bi * 512: oc * D + (bi + 1) * 512],
                            start=(oc == 0), stop=(oc == 31), skip_group_check=True)), waits=wts)
                        wts = []
                        if oc == 31:
                            t_dmm.append(t_d)
                t_down = t_dmm[-1]
                for bi in (0, 1):
                    wts = [t_dmm[bi], t_rhd]
                    if t_tmp2[p]:
                        wts.append(t_tmp2[p])
                    t_gevac[bi] = P.c("s", (lambda e, p=p, bi=bi: e.activation(
                        gt[p][:, bi * 512:(bi + 1) * 512], pb[6 + bi][:, :],
                        AF.Silu, scale=cols[p][:, 20:21])), waits=wts)
                wts = [t_qevac[0], t_qevac[1]]
                if t_tmp2[p]:
                    wts.append(t_tmp2[p])
                t_t1 = P.c("v", (lambda e, p=p, tc=tc: e.tensor_mul(
                    tmp[p][:, :], qt[p][:, :], st_used[:, tc * D:(tc + 1) * D])),
                    waits=wts)
                t_t2 = P.c("g", (lambda e, p=p: e.tensor_mul(
                    tmp[p][:, :], tmp[p][:, :], gt[p][:, :])),
                    waits=[t_t1, t_gevac[0], t_gevac[1]])
                t_tmp2[p] = t_t2
                t_cb = P.c("v", (lambda e, p=p: e.tensor_add(
                    xb[p][:, :], tmp[p][:, :], xb[p][:, :])),
                    waits=[t_t2, t_xnact[p]])
                t_comb[p] = t_cb
                t_outd[p] = P.d("y", (lambda e, b=b, tc=tc, p=p: e.dma_start(
                    out=d_out[b, tc * 128:(tc + 1) * 128, :], in_=xb[p][:, :])),
                    "ou%d" % p, waits=[t_cb])

            fin = [P.total("ou0"), P.total("ou1"), ("cc", 1)]
            P.add("y", lambda e: e.nop(), waits=fin, sem=None)
            P.add("g", lambda e: e.nop(), waits=fin, sem=None)

        for name in sorted(set(P.cnt.keys()) | {"ct", "cv", "cs", "cg", "cc"}):
            sems[name] = es.enter_context(nc.semaphore(name))
        with nc.Block() as block:
            def mk(eng_key):
                def body(eng):
                    seen = {}
                    for waits, fn, sem, amt in P.q[eng_key]:
                        for (s, v) in waits:
                            if seen.get(s, -1) < v:
                                eng.wait_ge(sems[s], v)
                                seen[s] = v
                        if sem == "cc":
                            with allow_ici_all_to_all():
                                inst = fn(eng)
                        else:
                            inst = fn(eng)
                        if sem is not None:
                            inst.then_inc(sems[sem], amt)
                return body
            block.sync(mk("y"))
            block.vector(mk("v"))
            block.scalar(mk("s"))
            block.tensor(mk("t"))
            block.gpsimd(mk("g"))
    return nc


_NC_CACHE = None


def kernel(L0_attn_x, L0_attn_Wq, L0_attn_Wk, L0_attn_Wv,
           L0_ffn_w_gate, L0_ffn_w_up, L0_ffn_w_down):
    global _NC_CACHE
    x = np.ascontiguousarray(np.asarray(L0_attn_x, dtype=np.float32))
    wq = np.ascontiguousarray(np.asarray(L0_attn_Wq, dtype=np.float32))
    wk = np.ascontiguousarray(np.asarray(L0_attn_Wk, dtype=np.float32))
    wv = np.ascontiguousarray(np.asarray(L0_attn_Wv, dtype=np.float32))
    wgT = np.ascontiguousarray(np.asarray(L0_ffn_w_gate, dtype=np.float32).T)
    wuT = np.ascontiguousarray(np.asarray(L0_ffn_w_up, dtype=np.float32).T)
    wdT = np.ascontiguousarray(np.asarray(L0_ffn_w_down, dtype=np.float32).T)

    if _NC_CACHE is None:
        _NC_CACHE = build_nc()
    nc = _NC_CACHE

    in_maps = []
    for c in range(NCORES):
        oh = np.zeros((1, 4), np.float32)
        oh[0, c // 2] = 1.0
        in_maps.append({
            "x": np.ascontiguousarray(x[:, TL * c:TL * (c + 1), :]),
            "wq": wq, "wk": wk, "wv": wv,
            "wgT": wgT, "wuT": wuT, "wdT": wdT,
            "oh": oh,
        })
    res = run_bass_kernel_spmd(nc, in_maps, core_ids=list(range(NCORES)))
    out = np.concatenate(
        [np.asarray(res.results[c]["out"], dtype=np.float32)
         for c in range(NCORES)], axis=1)
    return out


# version 07
# speedup vs baseline: 1.2025x; 1.2025x over previous
import contextlib
import numpy as np

import concourse.bass as bass
import concourse.mybir as mybir
from concourse.bass_utils import run_bass_kernel_spmd
from concourse.aot_env import allow_ici_all_to_all

F32 = mybir.dt.float32
BF16 = mybir.dt.bfloat16
I32 = mybir.dt.int32
F8 = mybir.dt.float8e4
ALU = mybir.AluOpType
AF = mybir.ActivationFunctionType
AX = mybir.AxisListType.X

NCORES = 8
B = 16
TL = 512
D = 1024
NI = 4096
NT = B * TL // 128
GAMMA = 0.96875
EPS = 1e-5

CH = {"t": "ct", "v": "cv", "s": "cs", "g": "cg"}
SERIAL = ("v", "s", "g")


class Plan:

    def __init__(self):
        self.q = {e: [] for e in "tvsgy"}
        self.cnt = {}
        self.pending = {e: [] for e in "tvsgy"}

    def tick(self, sem, amt):
        self.cnt[sem] = self.cnt.get(sem, 0) + amt
        return (sem, self.cnt[sem])

    def add(self, eng, fn, waits=(), sem=None, amt=1):
        w = list(self.pending[eng]) + list(waits)
        self.pending[eng] = []
        self.q[eng].append((tuple(w), fn, sem, amt))
        return self.tick(sem, amt) if sem else None

    def c(self, eng, fn, waits=()):
        w = list(waits)
        if eng in SERIAL and self.cnt.get(CH[eng], 0) > 0:
            w.append((CH[eng], self.cnt[CH[eng]]))
        return self.add(eng, fn, w, CH[eng], 1)

    def d(self, eng, fn, stream, waits=()):
        return self.add(eng, fn, waits, "D" + stream, 16)

    def cq(self, eng, fn, waits=()):
        return self.add(eng, fn, waits, None)

    def total(self, stream):
        return ("D" + stream, self.cnt.get("D" + stream, 0))

    def barrier(self, tickets):
        for e in self.q:
            self.pending[e].extend(tickets)


def build_nc():
    nc = bass.Bass("TRN2", target_bir_lowering=False)
    d_x = nc.dram_tensor("x", [B, TL, D], F32, kind="ExternalInput").ap()
    d_wq = nc.dram_tensor("wq", [D, D], F32, kind="ExternalInput").ap()
    d_wk = nc.dram_tensor("wk", [D, D], F32, kind="ExternalInput").ap()
    d_wv = nc.dram_tensor("wv", [D, D], F32, kind="ExternalInput").ap()
    d_wgT = nc.dram_tensor("wgT", [D, NI], F32, kind="ExternalInput").ap()
    d_wuT = nc.dram_tensor("wuT", [D, NI], F32, kind="ExternalInput").ap()
    d_wdT = nc.dram_tensor("wdT", [NI, D], F32, kind="ExternalInput").ap()
    d_oh = nc.dram_tensor("oh", [1, 4], F32, kind="ExternalInput").ap()
    d_out = nc.dram_tensor("out", [B, TL, D], F32, kind="ExternalOutput").ap()
    h_ccin = nc.dram_tensor("cc_in", [4096, TL], BF16)
    h_ccout = nc.dram_tensor("cc_out", [4096, TL], BF16)
    h_scr8 = nc.dram_tensor("scr8", [8, 128], F32)
    d_ccin = h_ccin.ap()
    d_ccout = h_ccout.ap()

    P = Plan()
    es = contextlib.ExitStack()
    with es:
        sb = lambda n, s, d, **k: es.enter_context(nc.sbuf_tensor(n, s, d, **k))
        tg = sb("tg", [128, 8 * NI], F8)
        tu = sb("tu", [128, 8 * NI], F8)
        td = sb("td", [128, 32 * D], F8)
        wqb = sb("wqb", [128, 8 * D], BF16)
        st_used = sb("st_used", [128, 4 * D], BF16)
        mat_cols = sb("mat_cols", [128, 12], F32)
        epscol = sb("epscol", [128, 1], F32)

        pb = [es.enter_context(nc.psum_tensor("pb%d" % i, [128, 512], F32))
              for i in range(8)]
        sems = {}

        es1 = contextlib.ExitStack()
        with es1:
            sb1 = lambda n, s, d, **k: es1.enter_context(nc.sbuf_tensor(n, s, d, **k))
            wvs_bc = sb1("wvs_bc", [128, D], F32)
            ident = sb1("ident", [128, 128], F32)
            oh_col = sb1("oh_col", [128, 4], F32)
            ones1 = sb1("ones1", [1, 128], F32)
            onescol = sb1("onescol", [128, 1], F32)
            ii32 = sb1("ii32", [128, 128], I32)
            rwsw_row = sb1("rwsw_row", [1, 12], F32)
            ohrow = sb1("ohrow", [1, 4], F32)
            row_sb = sb1("row_sb", [1, D], F32)
            gam512 = sb1("gam512", [128, 512], BF16)
            sum11 = sb1("sum11", [1, 4], F32)
            xz = [sb1("xz%d" % i, [128, D], F32) for i in range(2)]
            xnz = sb1("xnz", [128, D], F32)
            vtmp = sb1("vtmp", [128, D], F32)
            zacc = sb1("zacc", [128, 4 * D], F32)
            wld = [sb1("wld%d" % i, [128, D], F32) for i in range(2)]
            wqp = sb1("wqp", [128, D], F32)
            gqy = sb1("gqy", [128, D], F32)
            gqp = sb1("gqp", [128, D], F32)
            wsumV = sb1("wsumV", [128, 1], F32)
            zT = sb1("zT", [128, 8 * 512], BF16)
            wkb = [sb1("wkb%d" % i, [128, D], BF16) for i in range(2)]
            mTp = [sb1("mTp%d" % i, [128, 512], BF16) for i in range(2)]
            mTw = [sb1("mTw%d" % i, [128, 512], BF16) for i in range(2)]
            STp = [sb1("STp%d" % i, [128, 512], F32) for i in range(2)]
            carry = sb1("carry", [128, 1], F32)
            zcols = sb1("zcols", [128, 24], F32)
            wvs_st = sb1("wvs_st", [128, 8], F32)
            wqy = vtmp
            sqscr = gqp

            t_eps = P.c("v", lambda e: e.memset(epscol[:, :], EPS))
            P.c("v", lambda e: e.memset(rwsw_row[:, :], 0.0))
            t_ones1 = P.c("v", lambda e: e.memset(ones1[:, :], 1.0))
            t_onesc = P.c("v", lambda e: e.memset(onescol[:, :], 1.0))
            P.c("v", lambda e: e.memset(gam512[:, :], GAMMA))
            t_z0 = P.c("v", lambda e: e.memset(zacc[:, :], 0.0))
            P.c("v", lambda e: e.memset(wsumV[:, :], 0.0))
            t_ii = P.c("g", lambda e: e.iota(ii32[:, :], [[1, 128]], base=0,
                                             channel_multiplier=-1))
            t_id = P.c("g", lambda e: e.tensor_scalar(
                ident[:, :], ii32[:, :], 0.0, None, ALU.is_equal), waits=[t_ii])
            t_ohl = P.d("y", lambda e: e.dma_start(out=ohrow[:, :], in_=d_oh),
                        "oh")

            lastc = {0: None, 1: None}
            for j in range(8):
                p = j % 2
                w = [lastc[p]] if lastc[p] else []
                t_l = P.d("y", (lambda e, j=j, p=p: e.dma_start(
                    out=wld[p][:, :], in_=d_wq[j * 128:(j + 1) * 128, :])),
                    "w%d" % p, waits=w)
                lastc[p] = P.c("v", (lambda e, j=j, p=p: e.tensor_copy(
                    wqb[:, j * D:(j + 1) * D], wld[p][:, :])), waits=[t_l])
            for j in range(8):
                p = j % 2
                t_l = P.d("y", (lambda e, j=j, p=p: e.dma_start(
                    out=wld[p][:, :], in_=d_wv[j * 128:(j + 1) * 128, :])),
                    "w%d" % p, waits=[lastc[p]])
                lastc[p] = P.c("v", (lambda e, j=j, p=p: e.reduce_sum(
                    wvs_st[:, j:j + 1], wld[p][:, :], AX)), waits=[t_l])
            t_s8 = P.d("y", lambda e: e.dma_start(
                out=h_scr8.ap().rearrange("a b -> b a"), in_=wvs_st[:, :],
                allow_slow_non_contiguous=True),
                "s8", waits=[lastc[0], lastc[1]])
            t_row = P.d("y", lambda e: e.dma_start(
                out=row_sb[:, :], in_=h_scr8.ap().rearrange("a b -> (a b)")[None, :]),
                "row", waits=[t_s8])
            t_b1 = P.c("t", lambda e: e.matmul(pb[0][:, :], ones1[0:1, :],
                                               row_sb[0:1, 0:512], start=True,
                                               stop=True), waits=[t_row, t_ones1])
            t_b2 = P.c("t", lambda e: e.matmul(pb[1][:, :], ones1[0:1, :],
                                               row_sb[0:1, 512:1024], start=True,
                                               stop=True), waits=[t_row, t_ones1])
            t_wbc1 = P.c("s", lambda e: e.copy(wvs_bc[:, 0:512], pb[0][:, :]),
                         waits=[t_b1])
            t_wbc2 = P.c("s", lambda e: e.copy(wvs_bc[:, 512:1024], pb[1][:, :]),
                         waits=[t_b2])
            t_b3 = P.c("t", lambda e: e.matmul(pb[2][:, 0:4], ones1[0:1, :],
                                               ohrow[0:1, :], start=True, stop=True, skip_group_check=True),
                       waits=[t_ohl, t_ones1])
            t_ohc = P.c("s", lambda e: e.copy(oh_col[:, :], pb[2][:, 0:4]),
                        waits=[t_b3])

            mats = [(d_wgT, 8, 4), (d_wuT, 8, 4), (d_wdT, 32, 1)]
            accV, accG = t_z0, None
            t_s11_prev = None
            t_sw = None
            for mi, (dm, nj, nq) in enumerate(mats):
                for ci in range(nj * nq):
                    j, qq = divmod(ci, nq)
                    p = ci % 2
                    t_l = P.d("y", (lambda e, dm=dm, j=j, qq=qq, p=p: e.dma_start(
                        out=wld[p][:, :],
                        in_=dm[j * 128:(j + 1) * 128, qq * D:(qq + 1) * D])),
                        "w%d" % p, waits=[lastc[p]] if lastc[p] else [])
                    scr = vtmp if p == 0 else gqy
                    t_r = P.c("v", (lambda e, p=p, scr=scr: e.reduce_sum(
                        scr[:, 0:1], wld[p][:, :], AX, apply_absolute_value=True)),
                        waits=[t_l])
                    t_a = P.c("v", (lambda e, scr=scr: e.tensor_add(
                        wsumV[:, :], wsumV[:, :], scr[:, 0:1])), waits=[t_r])
                    lastc[p] = t_r
                    accV = t_a
                t_cmb = accV
                wmm = [t_cmb, t_onesc]
                if t_s11_prev:
                    wmm.append(t_s11_prev)
                t_mm = P.c("t", (lambda e, mi=mi: e.matmul(
                    pb[6][0:1, mi:mi + 1], wsumV[:, 0:1], onescol[:, :],
                    start=True, stop=True, skip_group_check=True)), waits=wmm)
                t_s11 = P.c("v", (lambda e, mi=mi: e.tensor_copy(
                    sum11[0:1, mi:mi + 1], pb[6][0:1, mi:mi + 1])), waits=[t_mm])
                t_s11_prev = t_s11
                t_rw = P.c("v", (lambda e, mi=mi: e.tensor_scalar(
                    rwsw_row[0:1, 3 * mi:3 * mi + 1], sum11[0:1, mi:mi + 1],
                    float(2.0 ** -22), 1e-5, ALU.mult, ALU.max)), waits=[t_s11])
                P.c("v", (lambda e, mi=mi: e.reciprocal(
                    rwsw_row[0:1, 3 * mi + 1:3 * mi + 2],
                    rwsw_row[0:1, 3 * mi:3 * mi + 1])), waits=[t_rw])
                t_sw = P.c("v", (lambda e, mi=mi: e.tensor_scalar(
                    rwsw_row[0:1, 3 * mi + 2:3 * mi + 3],
                    rwsw_row[0:1, 3 * mi + 1:3 * mi + 2], -1.0, None, ALU.mult)))
                accV = P.c("v", lambda e: e.memset(wsumV[:, :], 0.0), waits=[t_mm])
            t_bmm = P.c("t", lambda e: e.matmul(
                pb[2][:, 0:12], ones1[0:1, :], rwsw_row[0:1, :],
                start=True, stop=True, skip_group_check=True), waits=[t_sw, t_ohc])
            t_mcols = P.c("s", lambda e: e.copy(mat_cols[:, :], pb[2][:, 0:12]),
                          waits=[t_bmm])

            for mi, (dm, nj, nq) in enumerate(mats):
                dst = (tg, tu, td)[mi]
                for ci in range(nj * nq):
                    j, qq = divmod(ci, nq)
                    p = ci % 2
                    t_l = P.d("y", (lambda e, dm=dm, j=j, qq=qq, p=p: e.dma_start(
                        out=wld[p][:, :],
                        in_=dm[j * 128:(j + 1) * 128, qq * D:(qq + 1) * D])),
                        "w%d" % p, waits=[lastc[p]] if lastc[p] else [])
                    P.c("v", (lambda e, p=p, mi=mi: e.tensor_scalar(
                        wqp[:, :], wld[p][:, :],
                        mat_cols[:, 3 * mi + 1:3 * mi + 2],
                        0.5, ALU.mult, ALU.is_gt)), waits=[t_l, t_mcols])
                    t2 = P.c("v", (lambda e, p=p, mi=mi: e.tensor_scalar(
                        wqy[:, :], wld[p][:, :],
                        mat_cols[:, 3 * mi + 2:3 * mi + 3],
                        0.5, ALU.mult, ALU.is_gt)))
                    if nq == 4:
                        sl = (lambda dst=dst, j=j, qq=qq:
                              dst[:, j * NI + qq * D: j * NI + (qq + 1) * D])
                    else:
                        sl = (lambda dst=dst, j=j: dst[:, j * D:(j + 1) * D])
                    P.c("v", (lambda e, sl=sl: e.tensor_sub(
                        sl(), wqp[:, :], wqy[:, :])))
                    lastc[p] = t2

            t_sq = None
            t_zstt = t_z0
            xread = {0: None, 1: None}
            for n in range(NT):
                b, tc = divmod(n, 4)
                p = n % 2
                w = [xread[p]] if xread[p] else []
                t_xl = P.d("y", (lambda e, b=b, tc=tc, p=p: e.dma_start(
                    out=xz[p][:, :], in_=d_x[b, tc * 128:(tc + 1) * 128, :])),
                    "xz%d" % p, waits=w)
                t_s1 = P.c("v", (lambda e, p=p: e.reduce_sum(
                    zcols[:, 0:1], xz[p][:, :], AX)), waits=[t_xl])
                t_s2 = P.c("s", (lambda e, p=p: e.activation(
                    sqscr[:, :], xz[p][:, :], AF.Square,
                    accum_out=zcols[:, 7:8])),
                    waits=[t_xl, ("cg", P.cnt["cg"])])
                P.c("v", lambda e: e.tensor_scalar(
                    zcols[:, 1:2], zcols[:, 0:1], float(1.0 / D), None, ALU.mult),
                    waits=[t_s1])
                P.c("v", lambda e: e.tensor_mul(
                    zcols[:, 2:3], zcols[:, 1:2], zcols[:, 1:2]))
                t_vr = P.c("v", lambda e: e.scalar_tensor_tensor(
                    zcols[:, 3:4], zcols[:, 7:8], float(1.0 / D), zcols[:, 2:3],
                    ALU.mult, ALU.subtract), waits=[t_s2])
                t_sd = P.c("s", lambda e: e.activation(
                    zcols[:, 4:5], zcols[:, 3:4], AF.Sqrt,
                    bias=epscol[:, 0:1]), waits=[t_vr, t_eps])
                t_rs = P.c("v", lambda e: e.reciprocal(
                    zcols[:, 5:6], zcols[:, 4:5]), waits=[t_sd])
                t_nm = P.c("v", lambda e: e.scalar_tensor_tensor(
                    zcols[:, 6:7], zcols[:, 1:2], -1.0, zcols[:, 5:6],
                    ALU.mult, ALU.mult))
                t_xn = P.c("s", (lambda e, p=p: e.activation(
                    xnz[:, :], xz[p][:, :], AF.Identity,
                    bias=zcols[:, 6:7], scale=zcols[:, 5:6])),
                    waits=[t_nm, t_zstt])
                xread[p] = t_xn
                P.c("v", lambda e: e.tensor_mul(
                    vtmp[:, :], xnz[:, :], wvs_bc[:, :]),
                    waits=[t_xn, t_wbc1, t_wbc2])
                t_vs = P.c("v", lambda e: e.reduce_sum(
                    zcols[:, 8:9], vtmp[:, :], AX))
                t_zstt = P.c("v", (lambda e, tc=tc: e.scalar_tensor_tensor(
                    zacc[:, tc * D:(tc + 1) * D], xnz[:, :], zcols[:, 8:9],
                    zacc[:, tc * D:(tc + 1) * D], ALU.mult, ALU.add)), waits=[t_vs])

            t_ev = {0: None, 1: None}
            for j in range(8):
                for tc in range(4):
                    bk = (j * 4 + tc) % 2
                    w = [t_zstt, t_id, t_wbc1, t_wbc2]
                    if t_ev[bk]:
                        w.append(t_ev[bk])
                    t_tr = P.c("t", (lambda e, j=j, tc=tc, bk=bk: e.transpose(
                        pb[bk][:, 0:128],
                        zacc[:, tc * D + j * 128: tc * D + (j + 1) * 128],
                        ident[:, :])), waits=w)
                    t_ev[bk] = P.c("v", (lambda e, j=j, tc=tc, bk=bk: e.tensor_copy(
                        zT[:, j * 512 + tc * 128: j * 512 + (tc + 1) * 128],
                        pb[bk][:, 0:128])), waits=[t_tr])
            zt_all = [t_ev[0], t_ev[1]]
            mm_last = []
            wkc = {0: None, 1: None}
            t_m = None
            for j in range(8):
                p = j % 2
                t_l = P.d("y", (lambda e, j=j, p=p: e.dma_start(
                    out=wld[p][:, :], in_=d_wk[j * 128:(j + 1) * 128, :])),
                    "w%d" % p, waits=[lastc[p]] if lastc[p] else [])
                wc = [t_l]
                if wkc[p]:
                    wc.append(wkc[p])
                t_c = P.c("v", (lambda e, p=p: e.tensor_copy(
                    wkb[p][:, :], wld[p][:, :])), waits=wc)
                lastc[p] = t_c
                for k in range(8):
                    wts = [t_c]
                    if j == 0:
                        wts += zt_all + [t_mcols, t_s11_prev]
                    fn = (lambda e, j=j, k=k, p=p: e.matmul(
                        pb[k][:, :], wkb[p][:, k * 128:(k + 1) * 128],
                        zT[:, j * 512:(j + 1) * 512],
                        start=(j == 0), stop=(j == 7), skip_group_check=True))
                    if j == 7 or k == 7:
                        t_m = P.c("t", fn, waits=wts)
                        if j == 7:
                            mm_last.append(t_m)
                    else:
                        P.cq("t", fn, waits=wts)
                wkc[p] = t_m
            ccin_dmas = []
            mtp_free = {0: None, 1: None}
            for k in range(8):
                p = k % 2
                w = [mm_last[k]]
                if mtp_free[p]:
                    w.append(mtp_free[p])
                t_e = P.c("v", (lambda e, k=k, p=p: e.tensor_copy(
                    mTp[p][:, :], pb[k][:, :])), waits=w)
                last_d = None
                for ji in range(4):
                    jdst = (k // 4) + 2 * ji
                    rr = 512 * jdst + (128 * k) % 512
                    last_d = P.d("y", (lambda e, rr=rr, p=p: e.dma_start(
                        out=d_ccin[rr:rr + 128, :], in_=mTp[p][:, :])),
                        "ci%d" % p, waits=[t_e])
                mtp_free[p] = last_d
                ccin_dmas.append(last_d)
            t_cc = P.add("g", (lambda e: e.collective_compute(
                "AllToAll", ALU.bypass,
                replica_groups=[list(range(NCORES))],
                ins=[d_ccin.opt()], outs=[d_ccout.opt()])),
                waits=ccin_dmas + [("cg", P.cnt["cg"])], sem="cc", amt=1)

            sel_last = None
            mtw_free = {0: None, 1: None}
            stp_free = {0: None, 1: None}
            for k in range(4):
                t_cy = P.c("v", lambda e: e.memset(carry[:, :], 0.0))
                for cp in range(8):
                    p = cp % 2
                    w = [t_cc]
                    if mtw_free[p]:
                        w.append(mtw_free[p])
                    t_l = P.d("y", (lambda e, k=k, cp=cp, p=p: e.dma_start(
                        out=mTw[p][:, :],
                        in_=d_ccout[512 * cp + 128 * k: 512 * cp + 128 * k + 128, :])),
                        "mw%d" % p, waits=w)
                    wscan = [t_l, t_cy]
                    if cp == 0:
                        wscan.append(P.c("v", (lambda e, p=p: e.memset(
                            mTw[p][:, 0:1], 0.0)), waits=[t_l]))
                    if stp_free[p]:
                        wscan.append(stp_free[p])
                    t_sc = P.c("v", (lambda e, p=p: e.tensor_tensor_scan(
                        STp[p][:, :], gam512[:, :], mTw[p][:, :], carry[:, 0:1],
                        ALU.mult, ALU.add)), waits=wscan)
                    mtw_free[p] = t_sc
                    t_cy = P.c("v", (lambda e, p=p: e.tensor_copy(
                        carry[:, :], STp[p][:, 511:512])), waits=[t_sc])
                    if cp == 0:
                        P.c("v", (lambda e, p=p: e.tensor_copy(
                            STp[p][:, 0:1], STp[p][:, 1:2])), waits=[t_sc])
                    dst = (lambda k=k, cp=cp:
                           st_used[:, k * D + 128 * cp: k * D + 128 * cp + 128])
                    t_sel = P.c("v", (lambda e, p=p, dst=dst: e.tensor_scalar(
                        dst(), STp[p][:, 0::4], oh_col[:, 0:1], None, ALU.mult)),
                        waits=[t_ohc])
                    for qq in range(1, 4):
                        t_sel = P.c("v", (lambda e, p=p, qq=qq, dst=dst:
                                          e.scalar_tensor_tensor(
                            dst(), STp[p][:, qq::4], oh_col[:, qq:qq + 1], dst(),
                            ALU.mult, ALU.add)))
                    stp_free[p] = t_sel
                    sel_last = t_sel

            barrier = [("cv", P.cnt["cv"]), ("cs", P.cnt["cs"]),
                       ("ct", P.cnt["ct"]), ("cg", P.cnt["cg"]), ("cc", 1)]
            barrier += [P.total(s) for s in
                        ["oh", "s8", "row", "w0", "w1", "xz0", "xz1",
                         "ci0", "ci1", "mw0", "mw1"]]
            P.barrier(barrier)

        es2 = contextlib.ExitStack()
        with es2:
            sb2 = lambda n, s, d, **k: es2.enter_context(nc.sbuf_tensor(n, s, d, **k))
            xb = [sb2("xb%d" % i, [128, D], F32) for i in range(2)]
            xn = [sb2("xn%d" % i, [128, D], F32) for i in range(2)]
            xnb = [sb2("xnb%d" % i, [128, D], BF16) for i in range(2)]
            xnT = [sb2("xnT%d" % i, [128, D], BF16, is_transpose_dest=True)
                   for i in range(2)]
            kq = [sb2("kq%d" % i, [128, D], BF16) for i in range(2)]
            kqT = [sb2("kqT%d" % i, [128, D], BF16, is_transpose_dest=True)
                   for i in range(2)]
            qt = [sb2("qt%d" % i, [128, D], BF16) for i in range(2)]
            gt = [sb2("gt%d" % i, [128, D], BF16) for i in range(2)]
            tmp = [sb2("tmp%d" % i, [128, D], BF16) for i in range(2)]
            hh = sb2("hh", [128, NI], BF16)
            kh = sb2("kh", [128, NI], BF16)
            khT = sb2("khT", [128, NI], BF16, is_transpose_dest=True)
            ysc = sb2("ysc", [128, D], F32)
            isc = sb2("isc", [128, D], I32)
            sq2 = ysc
            gsil = [sb2("gsil%d" % i, [128, 512], F32) for i in range(2)]
            cols = [sb2("cols%d" % i, [128, 24], F32) for i in range(2)]

            t_outd = {0: None, 1: None}
            t_sq2 = None
            t_comb = {0: None, 1: None}
            t_qevac = {0: None, 1: None}
            t_gevac = {0: None, 1: None}
            t_gsil = {}
            t_hstt = {}
            t_quantG = None
            t_quantV = None
            t_khT = None
            t_xnT = {0: None, 1: None}
            t_kqT = {0: None, 1: None}
            t_down = None
            t_tmp2 = {0: None, 1: None}
            t_xnact = {0: None, 1: None}
            qcons = {0: None, 1: None}
            gucons = {0: None, 1: None}

            for n in range(NT):
                b, tc = divmod(n, 4)
                p = n % 2
                w = [t_outd[p]] if t_outd[p] else []
                t_xl = P.d("y", (lambda e, b=b, tc=tc, p=p: e.dma_start(
                    out=xb[p][:, :], in_=d_x[b, tc * 128:(tc + 1) * 128, :])),
                    "xb%d" % p, waits=w)
                t_s1 = P.c("v", (lambda e, p=p: e.reduce_sum(
                    cols[p][:, 0:1], xb[p][:, :], AX)), waits=[t_xl])
                wsq = [t_xl]
                if t_quantV:
                    wsq.append(t_quantV)
                t_s2 = P.c("s", (lambda e, p=p: e.activation(
                    sq2[:, :], xb[p][:, :], AF.Square,
                    accum_out=cols[p][:, 7:8])), waits=wsq)
                t_sq2 = t_s2
                P.c("v", (lambda e, p=p: e.tensor_scalar(
                    cols[p][:, 1:2], cols[p][:, 0:1], float(1.0 / D), None,
                    ALU.mult)), waits=[t_s1])
                P.c("v", (lambda e, p=p: e.tensor_mul(
                    cols[p][:, 2:3], cols[p][:, 1:2], cols[p][:, 1:2])))
                t_vr = P.c("v", (lambda e, p=p: e.scalar_tensor_tensor(
                    cols[p][:, 3:4], cols[p][:, 7:8], float(1.0 / D),
                    cols[p][:, 2:3], ALU.mult, ALU.subtract)), waits=[t_s2])
                t_sd = P.c("s", (lambda e, p=p: e.activation(
                    cols[p][:, 4:5], cols[p][:, 3:4], AF.Sqrt,
                    bias=epscol[:, 0:1])), waits=[t_vr, t_eps])
                t_rs = P.c("v", (lambda e, p=p: e.reciprocal(
                    cols[p][:, 5:6], cols[p][:, 4:5])), waits=[t_sd])
                t_nm = P.c("v", (lambda e, p=p: e.scalar_tensor_tensor(
                    cols[p][:, 6:7], cols[p][:, 1:2], -1.0, cols[p][:, 5:6],
                    ALU.mult, ALU.mult)))
                wxn = [t_nm]
                if t_quantV:
                    wxn.append(t_quantV)
                t_xna = P.c("s", (lambda e, p=p: e.activation(
                    xn[p][:, :], xb[p][:, :], AF.Identity,
                    bias=cols[p][:, 6:7], scale=cols[p][:, 5:6])), waits=wxn)
                t_xnact[p] = t_xna
                wb = [t_xna]
                if t_xnT[p]:
                    wb.append(t_xnT[p])
                t_xnb = P.c("v", (lambda e, p=p: e.tensor_copy(
                    xnb[p][:, :], xn[p][:, :])), waits=wb)
                t_mx = P.c("v", (lambda e, p=p: e.reduce_max(
                    cols[p][:, 8:9], xn[p][:, :], AX, apply_absolute_value=True)),
                    waits=[t_xna])
                P.c("v", (lambda e, p=p: e.tensor_scalar(
                    cols[p][:, 9:10], cols[p][:, 8:9], 1e-5, None, ALU.max)),
                    waits=[t_mx])
                P.c("v", (lambda e, p=p: e.reciprocal(
                    cols[p][:, 10:11], cols[p][:, 9:10])))
                t_sx = P.c("v", (lambda e, p=p: e.tensor_scalar(
                    cols[p][:, 11:12], cols[p][:, 10:11], 127.0, None, ALU.mult)))
                P.c("v", (lambda e, p=p: e.tensor_scalar(
                    cols[p][:, 12:13], cols[p][:, 9:10], float(1.0 / 127.0), None,
                    ALU.mult)))
                t_rxg = P.c("v", (lambda e, p=p: e.tensor_mul(
                    cols[p][:, 13:14], cols[p][:, 12:13], mat_cols[:, 0:1])))
                t_rxu = P.c("v", (lambda e, p=p: e.tensor_mul(
                    cols[p][:, 14:15], cols[p][:, 12:13], mat_cols[:, 3:4])))
                wq1 = [t_sx]
                if t_quantV:
                    wq1.append(t_quantV)
                t_q1 = P.c("v", (lambda e, p=p: e.tensor_scalar(
                    ysc[:, :], xn[p][:, :], cols[p][:, 11:12], 0.5, ALU.mult,
                    ALU.add)), waits=wq1)
                t_q2 = P.c("v", lambda e: e.tensor_copy(isc[:, :], ysc[:, :]),
                           waits=[t_q1])
                t_q3 = P.c("v", lambda e: e.tensor_tensor(
                    ysc[:, :], isc[:, :], ysc[:, :], ALU.is_gt), waits=[t_q2])
                wq4 = [t_q3]
                if t_kqT[p]:
                    wq4.append(t_kqT[p])
                t_q4 = P.c("v", (lambda e, p=p: e.tensor_sub(
                    kq[p][:, :], isc[:, :], ysc[:, :])), waits=wq4)
                t_quantV = t_q4
                last_t = None
                for j in range(8):
                    wdt = [t_xnb]
                    if j == 0 and qcons[p]:
                        wdt.append(qcons[p])
                    last_t = P.d("s", (lambda e, p=p, j=j: e.dma_start(
                        out=xnT[p][:, j * 128:(j + 1) * 128],
                        in_=xnb[p][:, j * 128:(j + 1) * 128], transpose=True)),
                        "xt%d" % p, waits=wdt)
                t_xnT[p] = last_t
                for j in range(8):
                    wdt = [t_q4]
                    if j == 0 and gucons[p]:
                        wdt.append(gucons[p])
                    last_t = P.d("s", (lambda e, p=p, j=j: e.dma_start(
                        out=kqT[p][:, j * 128:(j + 1) * 128],
                        in_=kq[p][:, j * 128:(j + 1) * 128], transpose=True)),
                        "kq%d" % p, waits=wdt)
                t_kqT[p] = last_t
                t_qmm = []
                for j in range(8):
                    wts = []
                    if j == 0:
                        wts = [t_xnT[p]]
                        for bi in (0, 1):
                            if t_qevac[bi]:
                                wts.append(t_qevac[bi])
                    for bi in (0, 1):
                        fn = (lambda e, p=p, j=j, bi=bi: e.matmul(
                            pb[bi][:, :], xnT[p][:, j * 128:(j + 1) * 128],
                            wqb[:, j * D + bi * 512: j * D + (bi + 1) * 512],
                            start=(j == 0), stop=(j == 7), skip_group_check=True))
                        if j == 7:
                            t_qmm.append(P.c("t", fn, waits=wts))
                        else:
                            P.cq("t", fn, waits=wts)
                        wts = []
                qcons[p] = t_qmm[1]
                for bi in (0, 1):
                    wts = [t_qmm[bi]]
                    if t_comb[p]:
                        wts.append(t_comb[p])
                    t_qevac[bi] = P.c("s", (lambda e, p=p, bi=bi: e.copy(
                        qt[p][:, bi * 512:(bi + 1) * 512], pb[bi][:, :])),
                        waits=wts)
                for oc in range(8):
                    bg = 2 + (oc % 2)
                    bu = 4 + (oc % 2)
                    wts = [t_kqT[p]]
                    if t_gsil.get(oc % 2):
                        wts.append(t_gsil[oc % 2])
                    if t_hstt.get(oc % 2):
                        wts.append(t_hstt[oc % 2])
                    if oc < 2:
                        if t_quantG:
                            wts.append(t_quantG)
                        if t_quantV:
                            wts.append(t_quantV)
                    t_gu = None
                    for j in range(8):
                        P.cq("t", (lambda e, p=p, j=j, oc=oc, bg=bg: e.matmul(
                            pb[bg][:, :], kqT[p][:, j * 128:(j + 1) * 128],
                            tg[:, j * NI + oc * 512: j * NI + (oc + 1) * 512],
                            start=(j == 0), stop=(j == 7), skip_group_check=True)),
                            waits=wts if j == 0 else [])
                        fn = (lambda e, p=p, j=j, oc=oc, bu=bu: e.matmul(
                            pb[bu][:, :], kqT[p][:, j * 128:(j + 1) * 128],
                            tu[:, j * NI + oc * 512: j * NI + (oc + 1) * 512],
                            start=(j == 0), stop=(j == 7), skip_group_check=True))
                        if j == 7:
                            t_gu = P.c("t", fn)
                        else:
                            P.cq("t", fn)
                    t_gs = P.c("s", (lambda e, p=p, oc=oc, bg=bg: e.activation(
                        gsil[oc % 2][:, :], pb[bg][:, :], AF.Silu,
                        scale=cols[p][:, 13:14])), waits=[t_gu, t_rxg])
                    t_gsil[oc % 2] = t_gs
                    t_hs = P.c("v", (lambda e, p=p, oc=oc, bu=bu:
                                     e.scalar_tensor_tensor(
                        hh[:, oc * 512:(oc + 1) * 512], pb[bu][:, :],
                        cols[p][:, 14:15], gsil[oc % 2][:, :], ALU.mult,
                        ALU.mult)), waits=[t_gs, t_rxu])
                    t_hstt[oc % 2] = t_hs
                gucons[p] = t_gu
                t_mh = P.c("v", (lambda e, p=p: e.reduce_max(
                    cols[p][:, 15:16], hh[:, :], AX, apply_absolute_value=True)),
                    waits=[t_hstt[0], t_hstt[1]])
                P.c("v", (lambda e, p=p: e.tensor_scalar(
                    cols[p][:, 16:17], cols[p][:, 15:16], 1e-5, None, ALU.max)),
                    waits=[t_mh])
                P.c("v", (lambda e, p=p: e.reciprocal(
                    cols[p][:, 17:18], cols[p][:, 16:17])))
                t_sh = P.c("v", (lambda e, p=p: e.tensor_scalar(
                    cols[p][:, 18:19], cols[p][:, 17:18], 127.0, None, ALU.mult)))
                P.c("v", (lambda e, p=p: e.tensor_scalar(
                    cols[p][:, 19:20], cols[p][:, 16:17], float(1.0 / 127.0),
                    None, ALU.mult)))
                t_rhd = P.c("v", (lambda e, p=p: e.tensor_mul(
                    cols[p][:, 20:21], cols[p][:, 19:20], mat_cols[:, 6:7])))
                qticks = []
                for q in range(4):
                    wts = [t_sh]
                    if t_khT:
                        wts.append(t_khT)
                    a1 = P.c("v", (lambda e, p=p, q=q: e.tensor_scalar(
                        ysc[:, :], hh[:, q * D:(q + 1) * D], cols[p][:, 18:19],
                        0.5, ALU.mult, ALU.add)), waits=wts)
                    a2 = P.c("v", lambda e: e.tensor_copy(
                        isc[:, :], ysc[:, :]), waits=[a1])
                    a3 = P.c("v", lambda e: e.tensor_tensor(
                        ysc[:, :], isc[:, :], ysc[:, :], ALU.is_gt), waits=[a2])
                    a4 = P.c("v", (lambda e, q=q: e.tensor_sub(
                        kh[:, q * D:(q + 1) * D], isc[:, :], ysc[:, :])),
                        waits=[a3])
                    qticks.append(a4)
                t_quantG = qticks[1]
                t_quantV = qticks[3]
                last_t = None
                for j in range(32):
                    wts = [qticks[j // 8]]
                    if j == 0 and t_down:
                        wts.append(t_down)
                    last_t = P.d("y", (lambda e, j=j: e.dma_start(
                        out=khT[:, j * 128:(j + 1) * 128],
                        in_=kh[:, j * 128:(j + 1) * 128], transpose=True)),
                        "kh", waits=wts)
                t_khT = last_t
                t_dmm = []
                for oc in range(32):
                    wts = []
                    if oc == 0:
                        wts = [t_khT]
                        for bi in (0, 1):
                            if t_gevac[bi]:
                                wts.append(t_gevac[bi])
                    for bi in (0, 1):
                        fn = (lambda e, oc=oc, bi=bi: e.matmul(
                            pb[6 + bi][:, :], khT[:, oc * 128:(oc + 1) * 128],
                            td[:, oc * D + bi * 512: oc * D + (bi + 1) * 512],
                            start=(oc == 0), stop=(oc == 31), skip_group_check=True))
                        if oc == 31:
                            t_dmm.append(P.c("t", fn, waits=wts))
                        else:
                            P.cq("t", fn, waits=wts)
                        wts = []
                t_down = t_dmm[-1]
                for bi in (0, 1):
                    wts = [t_dmm[bi], t_rhd]
                    if t_tmp2[p]:
                        wts.append(t_tmp2[p])
                    t_gevac[bi] = P.c("s", (lambda e, p=p, bi=bi: e.activation(
                        gt[p][:, bi * 512:(bi + 1) * 512], pb[6 + bi][:, :],
                        AF.Silu, scale=cols[p][:, 20:21])), waits=wts)
                wts = [t_qevac[0], t_qevac[1]]
                if t_tmp2[p]:
                    wts.append(t_tmp2[p])
                t_t1 = P.c("v", (lambda e, p=p, tc=tc: e.tensor_mul(
                    tmp[p][:, :], qt[p][:, :], st_used[:, tc * D:(tc + 1) * D])),
                    waits=wts)
                t_t2 = P.c("v", (lambda e, p=p: e.tensor_mul(
                    tmp[p][:, :], tmp[p][:, :], gt[p][:, :])),
                    waits=[t_t1, t_gevac[0], t_gevac[1]])
                t_tmp2[p] = t_t2
                t_cb = P.c("v", (lambda e, p=p: e.tensor_add(
                    xb[p][:, :], tmp[p][:, :], xb[p][:, :])),
                    waits=[t_t2, t_xnact[p]])
                t_comb[p] = t_cb
                t_outd[p] = P.d("y", (lambda e, b=b, tc=tc, p=p: e.dma_start(
                    out=d_out[b, tc * 128:(tc + 1) * 128, :], in_=xb[p][:, :])),
                    "ou%d" % p, waits=[t_cb])

            fin = [P.total("ou0"), P.total("ou1"), ("cc", 1)]
            P.add("y", lambda e: e.nop(), waits=fin, sem=None)
            P.add("g", lambda e: e.nop(), waits=fin, sem=None)

        for name in sorted(set(P.cnt.keys()) | {"ct", "cv", "cs", "cg", "cc"}):
            sems[name] = es.enter_context(nc.semaphore(name))
        with nc.Block() as block:
            def mk(eng_key):
                def body(eng):
                    seen = {}
                    for waits, fn, sem, amt in P.q[eng_key]:
                        for (s, v) in waits:
                            if seen.get(s, -1) < v:
                                eng.wait_ge(sems[s], v)
                                seen[s] = v
                        if sem == "cc":
                            with allow_ici_all_to_all():
                                inst = fn(eng)
                        else:
                            inst = fn(eng)
                        if sem is not None:
                            inst.then_inc(sems[sem], amt)
                return body
            block.sync(mk("y"))
            block.vector(mk("v"))
            block.scalar(mk("s"))
            block.tensor(mk("t"))
            block.gpsimd(mk("g"))
    return nc


_NC_CACHE = None


def kernel(L0_attn_x, L0_attn_Wq, L0_attn_Wk, L0_attn_Wv,
           L0_ffn_w_gate, L0_ffn_w_up, L0_ffn_w_down):
    global _NC_CACHE
    x = np.ascontiguousarray(np.asarray(L0_attn_x, dtype=np.float32))
    wq = np.ascontiguousarray(np.asarray(L0_attn_Wq, dtype=np.float32))
    wk = np.ascontiguousarray(np.asarray(L0_attn_Wk, dtype=np.float32))
    wv = np.ascontiguousarray(np.asarray(L0_attn_Wv, dtype=np.float32))
    wgT = np.ascontiguousarray(np.asarray(L0_ffn_w_gate, dtype=np.float32).T)
    wuT = np.ascontiguousarray(np.asarray(L0_ffn_w_up, dtype=np.float32).T)
    wdT = np.ascontiguousarray(np.asarray(L0_ffn_w_down, dtype=np.float32).T)

    if _NC_CACHE is None:
        _NC_CACHE = build_nc()
    nc = _NC_CACHE

    in_maps = []
    for c in range(NCORES):
        oh = np.zeros((1, 4), np.float32)
        oh[0, c // 2] = 1.0
        in_maps.append({
            "x": np.ascontiguousarray(x[:, TL * c:TL * (c + 1), :]),
            "wq": wq, "wk": wk, "wv": wv,
            "wgT": wgT, "wuT": wuT, "wdT": wdT,
            "oh": oh,
        })
    res = run_bass_kernel_spmd(nc, in_maps, core_ids=list(range(NCORES)))
    out = np.concatenate(
        [np.asarray(res.results[c]["out"], dtype=np.float32)
         for c in range(NCORES)], axis=1)
    return out
